# version 01 (final)
# baseline (speedup 1.0000x reference)
import numpy as np
import os
import concourse.bass as bass
import concourse.mybir as mybir

DBG = set(os.environ.get('KDBG', '').split(','))
F32 = mybir.dt.float32
BF16 = mybir.dt.bfloat16
AF = mybir.ActivationFunctionType
ALU = mybir.AluOpType
AX = mybir.AxisListType

C = 1024
T = 8192
S = 2048
NSEQ = 4
NH = 16
DH = 64
NE = 8
F0 = 2048
F1 = 4096
CC = C // 128
TT = T // 512
EPS = 1e-5


class Sem:
    def __init__(self, handle):
        self.h = handle
        self.n = 0


class Env:
    def __init__(self, nc):
        self.nc = nc
        self.sems = {}
        self.bufs = {}
        self.consts = {}
        self.bar_k = 0

    def s(self, name):
        return self.sems[name]


def declare_sems(env, ctx):
    for name in ["TE", "VE", "SE", "GE", "BAR",
                 "L0", "L1", "S0", "S1", "P0", "X0", "X1", "GD"]:
        env.sems[name] = Sem(ctx.enter_context(env.nc.semaphore(name)))


def emit_barrier(env, block):
    env.bar_k += 1
    tgt = 5 * env.bar_k
    h = env.sems["BAR"].h

    def fc(e):
        e.drain()
        e.nop().then_inc(h, 1)
        e.wait_ge(h, tgt)

    def fd(e):
        e.nop().then_inc(h, 1)
        e.wait_ge(h, tgt)
    block.sync(fd)
    for reg in [block.tensor, block.vector, block.scalar, block.gpsimd]:
        reg(fc)


def alloc_arenas(env, ctx):
    nc = env.nc
    B = env.bufs
    B["AR_W"] = ctx.enter_context(nc.sbuf_tensor("AR_W", [128, 32768], BF16))
    B["AR_I"] = ctx.enter_context(nc.sbuf_tensor("AR_I", [128, 24576], BF16))
    B["AR_O"] = ctx.enter_context(nc.sbuf_tensor("AR_O", [128, 16384], F32))
    B["AR_S"] = ctx.enter_context(nc.sbuf_tensor("AR_S", [128, 2048], F32))
    for i in range(8):
        B[f"ps{i}"] = ctx.enter_context(nc.psum_tensor(f"ps{i}", [128, 512], F32))


def alloc_consts(env, ctx, extra_named=()):
    nc = env.nc
    cs = env.consts
    cs["ones_bf"] = ctx.enter_context(nc.sbuf_tensor("ones_bf", [128, 128], BF16))
    cs["ones_f32"] = ctx.enter_context(nc.sbuf_tensor("ones_f32", [1, 128], F32))
    cs["ident_bf"] = ctx.enter_context(nc.sbuf_tensor("ident_bf", [128, 128], BF16))
    cs["misc"] = ctx.enter_context(nc.sbuf_tensor("misc", [128, 16], F32))
    for name, shape, dt in extra_named:
        cs[name] = ctx.enter_context(nc.sbuf_tensor(name, list(shape), dt))


def emit_load_consts(env, block, dram_map):
    P0 = env.s("P0")

    def y(e):
        for name, ap in dram_map.items():
            h = env.consts[name]
            e.dma_start(out=h[:, :], in_=ap).then_inc(P0.h, 16)
            P0.n += 16
        e.wait_ge(P0.h, P0.n)
    block.sync(y)
    tgt = P0.n

    def w(e):
        e.wait_ge(P0.h, tgt)
    for reg in [block.tensor, block.vector, block.scalar, block.gpsimd]:
        reg(w)


def emit_ln(env, block, in_f32, out_bf, out_f32=None, gamma=None, beta=None):
    TE, VE, SE = (env.s(x) for x in ["TE", "VE", "SE"])
    Lsl = [env.s("L0"), env.s("L1")]
    Ssl = [env.s("S0"), env.s("S1")]
    te0, ve0, se0 = TE.n, VE.n, SE.n
    l0 = [Lsl[0].n, Lsl[1].n]
    s0 = [Ssl[0].n, Ssl[1].n]
    VI, SI, TI = 3, 2, 2
    n_store = CC * ((1 if out_bf is not None else 0) +
                    (1 if out_f32 is not None else 0))

    AR_W, AR_I, AR_O = env.bufs["AR_W"], env.bufs["AR_I"], env.bufs["AR_O"]
    xt = AR_O[:, 0:8192]
    xc = AR_O[:, 8192:12288]
    xn2 = AR_O[:, 12288:16384]
    xb = AR_I[:, 0:4096]
    xsq = AR_I[:, 4096:8192]
    st_b = AR_I[:, 8192:16384]
    st_f = AR_W.bitcast(F32)[:, 0:8192]
    stats = env.bufs["AR_S"][0:1, 0:2048]
    ps_s1 = env.bufs["ps0"]
    ps_s2 = env.bufs["ps1"]
    ps_b1 = env.bufs["ps2"]
    ps_b2 = env.bufs["ps3"]
    onesb = env.consts["ones_bf"]
    onesf = env.consts["ones_f32"]

    def y_sync(e):
        for tt in range(TT):
            if tt >= 2:
                e.wait_ge(VE.h, ve0 + (tt - 1) * VI)
            sl = tt % 2
            for cc in range(CC):
                e.dma_start(
                    out=xt[:, (sl * CC + cc) * 512:(sl * CC + cc + 1) * 512],
                    in_=in_f32[cc * 128:(cc + 1) * 128, tt * 512:(tt + 1) * 512],
                ).then_inc(Lsl[sl].h, 16)
        Lsl[0].n += (TT + 1) // 2 * CC * 16
        Lsl[1].n += TT // 2 * CC * 16

    def t_tensor(e):
        for tt in range(TT):
            e.wait_ge(VE.h, ve0 + tt * VI + 1)
            e.wait_ge(SE.h, se0 + tt * SI + 1)
            for cc in range(CC):
                e.matmul(ps_s1[0:1, :], onesb[:, 0:1], xb[:, cc * 512:(cc + 1) * 512],
                         start=(cc == 0), stop=(cc == CC - 1))
            for cc in range(CC):
                mm = e.matmul(ps_s2[0:1, :], onesb[:, 0:1], xsq[:, cc * 512:(cc + 1) * 512],
                              start=(cc == 0), stop=(cc == CC - 1))
            mm.then_inc(TE.h, 1)
            e.wait_ge(SE.h, se0 + tt * SI + 2)
            e.matmul(ps_b1[:, :], onesf[0:1, :], stats[0:1, 0:512], start=True, stop=True)
            e.matmul(ps_b2[:, :], onesf[0:1, :], stats[0:1, 512:1024], start=True, stop=True
                     ).then_inc(TE.h, 1)
        TE.n += TT * TI

    def v_vector(e):
        for tt in range(TT):
            sl = tt % 2
            xts = xt[:, sl * CC * 512:(sl + 1) * CC * 512]
            e.wait_ge(Lsl[sl].h, l0[sl] + (tt // 2 + 1) * CC * 16)
            if tt >= 1:
                e.wait_ge(TE.h, te0 + (tt - 1) * TI + 1)
                e.wait_ge(SE.h, se0 + (tt - 1) * SI + 1)
            if tt >= 2:
                e.wait_ge(Ssl[sl].h, s0[sl] + (tt // 2) * n_store * 16)
            for cc in range(CC):
                ins = e.tensor_copy(xb[:, cc * 512:(cc + 1) * 512], xts[:, cc * 512:(cc + 1) * 512])
            ins.then_inc(VE.h, 1)
            e.wait_ge(TE.h, te0 + tt * TI + 1)
            e.tensor_scalar(stats[0:1, 0:512], ps_s1[0:1, :], 1.0 / C, None, ALU.mult)
            e.tensor_scalar(stats[0:1, 1024:1536], ps_s2[0:1, :], 1.0 / C, None, ALU.mult)
            e.drain()
            e.tensor_tensor(stats[0:1, 1536:2048], stats[0:1, 0:512], stats[0:1, 0:512], ALU.mult)
            e.drain()
            e.tensor_tensor(stats[0:1, 512:1024], stats[0:1, 1024:1536], stats[0:1, 1536:2048],
                            ALU.subtract).then_inc(VE.h, 1)
            e.wait_ge(TE.h, te0 + tt * TI + 2)
            for cc in range(CC):
                e.tensor_tensor(xc[:, cc * 512:(cc + 1) * 512],
                                xts[:, cc * 512:(cc + 1) * 512], ps_b1[:, :], ALU.subtract)
            e.drain()
            if gamma is None:
                for cc in range(CC):
                    ins = e.tensor_tensor(st_b[:, (sl * CC + cc) * 512:(sl * CC + cc + 1) * 512],
                                          xc[:, cc * 512:(cc + 1) * 512], ps_b2[:, :], ALU.mult)
                if out_f32 is not None:
                    for cc in range(CC):
                        ins = e.tensor_tensor(st_f[:, (sl * CC + cc) * 512:(sl * CC + cc + 1) * 512],
                                              xc[:, cc * 512:(cc + 1) * 512], ps_b2[:, :], ALU.mult)
            else:
                g = env.consts[gamma]
                b = env.consts[beta]
                for cc in range(CC):
                    e.tensor_tensor(xn2[:, cc * 512:(cc + 1) * 512],
                                    xc[:, cc * 512:(cc + 1) * 512], ps_b2[:, :], ALU.mult)
                e.drain()
                for cc in range(CC):
                    ins = e.tensor_scalar(st_b[:, (sl * CC + cc) * 512:(sl * CC + cc + 1) * 512],
                                          xn2[:, cc * 512:(cc + 1) * 512],
                                          g[:, cc:cc + 1], b[:, cc:cc + 1], ALU.mult, ALU.add)
                if out_f32 is not None:
                    for cc in range(CC):
                        ins = e.tensor_copy(st_f[:, (sl * CC + cc) * 512:(sl * CC + cc + 1) * 512],
                                            xn2[:, cc * 512:(cc + 1) * 512])
            ins.then_inc(VE.h, 1)
        VE.n += TT * VI

    def s_scalar(e):
        for tt in range(TT):
            sl = tt % 2
            e.wait_ge(VE.h, ve0 + tt * VI + 1)
            for cc in range(CC):
                ins = e.activation(xsq[:, cc * 512:(cc + 1) * 512],
                                   xb[:, cc * 512:(cc + 1) * 512], AF.Square)
            ins.then_inc(SE.h, 1)
            e.wait_ge(VE.h, ve0 + tt * VI + 2)
            e.activation(stats[0:1, 512:1024], stats[0:1, 512:1024], AF.Ln,
                         bias=env.consts["misc"][0:1, 0:1])
            e.drain()
            e.activation(stats[0:1, 512:1024], stats[0:1, 512:1024], AF.Exp, scale=-0.5
                         ).then_inc(SE.h, 1)
            e.wait_ge(VE.h, ve0 + (tt + 1) * VI)
            for cc in range(CC):
                if out_bf is not None:
                    e.dma_start(
                        out=out_bf[cc * 128:(cc + 1) * 128, tt * 512:(tt + 1) * 512],
                        in_=st_b[:, (sl * CC + cc) * 512:(sl * CC + cc + 1) * 512],
                    ).then_inc(Ssl[sl].h, 16)
                if out_f32 is not None:
                    e.dma_start(
                        out=out_f32[cc * 128:(cc + 1) * 128, tt * 512:(tt + 1) * 512],
                        in_=st_f[:, (sl * CC + cc) * 512:(sl * CC + cc + 1) * 512],
                    ).then_inc(Ssl[sl].h, 16)
        Ssl[0].n += (TT + 1) // 2 * n_store * 16
        Ssl[1].n += TT // 2 * n_store * 16
        SE.n += TT * SI

    block.sync(y_sync)
    block.tensor(t_tensor)
    block.vector(v_vector)
    block.scalar(s_scalar)


def emit_proj(env, block, in_bf, w_f32, cin, cout, epilogue, out,
              bias=None, resid=None, gate=None, w_is_bf16=False):
    TE, VE, SE = (env.s(x) for x in ["TE", "VE", "SE"])
    P0 = env.s("P0")
    X0 = env.s("X0")
    Lsl = [env.s("L0"), env.s("L1")]
    Ssl = [env.s("S0"), env.s("S1")]
    te0, ve0, se0, p00, x00 = TE.n, VE.n, SE.n, P0.n, X0.n
    l0 = [Lsl[0].n, Lsl[1].n]
    s0 = [Ssl[0].n, Ssl[1].n]
    CCi, CCo = cin // 128, cout // 128
    nslot_in = 1 if CCi > 16 else 2
    AR_W, AR_I, AR_O = env.bufs["AR_W"], env.bufs["AR_I"], env.bufs["AR_O"]
    assert CCi * cout <= 32768 and nslot_in * CCi * 512 <= 24576
    wsb = AR_W
    scalar_epi = epilogue in ("bf", "f32", "sigmoid_bf", "relusq_bf")
    EPI = SE if scalar_epi else VE
    out_f32 = epilogue in ("f32", "res", "res_bias", "gate_res")
    stg = AR_O if out_f32 else AR_O.bitcast(BF16)
    resid_t = AR_O[:, 8192:8192 + CCo * 512] if resid is not None else None
    gate_t = AR_O.bitcast(BF16)[:, 24576:24576 + CCo * 512] if gate is not None else None
    n_extra = (1 if resid is not None else 0) + (1 if gate is not None else 0)
    b_t = env.consts[bias] if bias is not None else None
    nb = max(1, min(CCi, 16384 // cout))
    nbatch = (CCi + nb - 1) // nb
    epi0 = se0 if scalar_epi else ve0 + CCi + 1

    def y_sync(e):
        for b in range(nbatch):
            chunks = list(range(b * nb, min((b + 1) * nb, CCi)))
            if b >= 1 and not w_is_bf16:
                e.wait_ge(VE.h, ve0 + b * nb)
            for ci in chunks:
                if w_is_bf16:
                    e.dma_start(out=wsb[:, ci * cout:(ci + 1) * cout],
                                in_=w_f32[ci * 128:(ci + 1) * 128, :]).then_inc(P0.h, 16)
                else:
                    sl = ci - b * nb
                    e.dma_start(out=AR_O[:, sl * cout:(sl + 1) * cout],
                                in_=w_f32[ci * 128:(ci + 1) * 128, :]).then_inc(P0.h, 16)
        P0.n += CCi * 16
        for tt in range(TT):
            sl = tt % nslot_in
            if tt >= nslot_in:
                e.wait_ge(TE.h, te0 + (tt - nslot_in + 1) * CCo)
            if n_extra:
                if tt >= 1:
                    e.wait_ge(EPI.h, epi0 + tt * CCo)
                else:
                    e.wait_ge(VE.h, ve0 + CCi + 1)
            for ci in range(CCi):
                e.dma_start(
                    out=AR_I[:, (sl * CCi + ci) * 512:(sl * CCi + ci + 1) * 512],
                    in_=in_bf[ci * 128:(ci + 1) * 128, tt * 512:(tt + 1) * 512],
                ).then_inc(Lsl[sl].h, 16)
            if resid is not None:
                for co in range(CCo):
                    e.dma_start(
                        out=resid_t[:, co * 512:(co + 1) * 512],
                        in_=resid[co * 128:(co + 1) * 128, tt * 512:(tt + 1) * 512],
                    ).then_inc(X0.h, 16)
            if gate is not None:
                for co in range(CCo):
                    e.dma_start(
                        out=gate_t[:, co * 512:(co + 1) * 512],
                        in_=gate[co * 128:(co + 1) * 128, tt * 512:(tt + 1) * 512],
                    ).then_inc(X0.h, 16)
        if nslot_in == 2:
            Lsl[0].n += (TT + 1) // 2 * CCi * 16
            Lsl[1].n += TT // 2 * CCi * 16
        else:
            Lsl[0].n += TT * CCi * 16
        X0.n += TT * n_extra * CCo * 16

    def v_weightcast(e):
        if w_is_bf16:
            e.wait_ge(P0.h, p00 + 16 * CCi)
            e.nop().then_inc(VE.h, CCi + 1)
            VE.n += CCi + 1
            return
        for b in range(nbatch):
            chunks = list(range(b * nb, min((b + 1) * nb, CCi)))
            e.wait_ge(P0.h, p00 + 16 * (chunks[-1] + 1))
            for ci in chunks:
                sl = ci - b * nb
                e.tensor_copy(wsb[:, ci * cout:(ci + 1) * cout],
                              AR_O[:, sl * cout:(sl + 1) * cout]).then_inc(VE.h, 1)
        e.drain()
        e.nop().then_inc(VE.h, 1)
        VE.n += CCi + 1

    def t_tensor(e):
        e.wait_ge(VE.h, ve0 + CCi + 1)
        for tt in range(TT):
            sl = tt % nslot_in
            e.wait_ge(Lsl[sl].h, l0[sl] + 16 * CCi * (tt // nslot_in + 1))
            for co in range(CCo):
                gidx = tt * CCo + co
                if gidx >= 2:
                    e.wait_ge(EPI.h, epi0 + gidx - 1)
                ps = env.bufs[f"ps{co % 2}"]
                for ci in range(CCi):
                    mm = e.matmul(ps[:, :],
                                  wsb[:, ci * cout + co * 128:ci * cout + (co + 1) * 128],
                                  AR_I[:, (sl * CCi + ci) * 512:(sl * CCi + ci + 1) * 512],
                                  start=(ci == 0), stop=(ci == CCi - 1))
                mm.then_inc(TE.h, 1)
        TE.n += TT * CCo

    def mk_epilogue(e, tt, co, dst, ps):
        if epilogue in ("bf", "f32"):
            if bias is None:
                return e.activation(dst, ps[:, :], AF.Identity)
            return e.activation(dst, ps[:, :], AF.Identity, bias=b_t[:, co:co + 1])
        if epilogue == "sigmoid_bf":
            return e.activation(dst, ps[:, :], AF.Sigmoid)
        if epilogue == "relusq_bf":
            tmp = env.bufs["AR_S"][:, 0:512]
            e.activation(tmp, ps[:, :], AF.Relu)
            e.drain()
            return e.activation(dst, tmp, AF.Square)
        if epilogue == "res":
            return e.tensor_tensor(dst, resid_t[:, co * 512:(co + 1) * 512],
                                   ps[:, :], ALU.add)
        if epilogue == "res_bias":
            return e.scalar_tensor_tensor(dst, ps[:, :], b_t[:, co:co + 1],
                                          resid_t[:, co * 512:(co + 1) * 512],
                                          ALU.add, ALU.add)
        if epilogue == "gate_res":
            gidx = tt * CCo + co
            tmp = env.bufs["AR_S"][:, (gidx % 2) * 512:(gidx % 2 + 1) * 512]
            e.drain()
            e.tensor_tensor(tmp, gate_t[:, co * 512:(co + 1) * 512], ps[:, :], ALU.mult)
            e.drain()
            return e.tensor_tensor(dst, resid_t[:, co * 512:(co + 1) * 512],
                                   tmp, ALU.add)
        raise ValueError(epilogue)

    def epi_vector(e):
        for tt in range(TT):
            ssl = tt % 2
            if n_extra:
                e.wait_ge(X0.h, x00 + 16 * n_extra * CCo * (tt + 1))
            if tt >= 2:
                e.wait_ge(Ssl[ssl].h, s0[ssl] + (tt // 2) * CCo * 16)
            for co in range(CCo):
                gidx = tt * CCo + co
                e.wait_ge(TE.h, te0 + gidx + 1)
                ps = env.bufs[f"ps{co % 2}"]
                dst = stg[:, (ssl * CCo + co) * 512:(ssl * CCo + co + 1) * 512]
                mk_epilogue(e, tt, co, dst, ps).then_inc(EPI.h, 1)
        EPI.n += TT * CCo

    def s_store(e):
        for tt in range(TT):
            ssl = tt % 2
            for co in range(CCo):
                gidx = tt * CCo + co
                e.wait_ge(EPI.h, epi0 + gidx + 1)
                src = stg[:, (ssl * CCo + co) * 512:(ssl * CCo + co + 1) * 512]
                e.dma_start(out=out[co * 128:(co + 1) * 128, tt * 512:(tt + 1) * 512],
                            in_=src).then_inc(Ssl[ssl].h, 16)
        Ssl[0].n += (TT + 1) // 2 * CCo * 16
        Ssl[1].n += TT // 2 * CCo * 16

    def s_both(e):
        for tt in range(TT):
            ssl = tt % 2
            if n_extra:
                e.wait_ge(X0.h, x00 + 16 * n_extra * CCo * (tt + 1))
            if tt >= 2:
                e.wait_ge(Ssl[ssl].h, s0[ssl] + (tt // 2) * CCo * 16)
            for co in range(CCo):
                gidx = tt * CCo + co
                e.wait_ge(TE.h, te0 + gidx + 1)
                ps = env.bufs[f"ps{co % 2}"]
                dst = stg[:, (ssl * CCo + co) * 512:(ssl * CCo + co + 1) * 512]
                mk_epilogue(e, tt, co, dst, ps).then_inc(EPI.h, 1)
                e.drain()
                e.dma_start(out=out[co * 128:(co + 1) * 128, tt * 512:(tt + 1) * 512],
                            in_=dst).then_inc(Ssl[ssl].h, 16)
        EPI.n += TT * CCo
        Ssl[0].n += (TT + 1) // 2 * CCo * 16
        Ssl[1].n += TT // 2 * CCo * 16

    block.sync(y_sync)
    block.vector(v_weightcast)
    block.tensor(t_tensor)
    if scalar_epi:
        block.scalar(s_both)
    else:
        block.vector(epi_vector)
        block.scalar(s_store)


def emit_const_casts(env, block, pairs):
    VE = env.s("VE")

    def v(e):
        for dst, srcn in pairs:
            e.tensor_copy(env.consts[dst][:, :], env.consts[srcn][:, :])
        e.drain()
        e.nop().then_inc(VE.h, 1)
        VE.n += 1
    block.vector(v)
    tgt = VE.n

    def w(e):
        e.wait_ge(VE.h, tgt)
    for reg in [block.sync, block.tensor, block.scalar, block.gpsimd]:
        reg(w)


def emit_vhat_router(env, block, in_bf, wv_f32, vh_out, cwt_out, renorm, xnf_in=None, with_v=True, rw_name="rw_f"):
    TE, VE, SE = (env.s(x) for x in ["TE", "VE", "SE"])
    P0 = env.s("P0")
    Lsl = [env.s("L0"), env.s("L1")]
    Ssl = [env.s("S0"), env.s("S1")]
    te0, ve0, se0, p00 = TE.n, VE.n, SE.n, P0.n
    l0 = [Lsl[0].n, Lsl[1].n]
    s0 = [Ssl[0].n, Ssl[1].n]
    AR_W, AR_I, AR_O, AR_S = (env.bufs[x] for x in ["AR_W", "AR_I", "AR_O", "AR_S"])
    wsb = AR_W
    NM = T // 128
    ob = AR_O.bitcast(BF16)
    vsb_off = 16384
    onesb = env.consts["ones_bf"]
    onesf = env.consts["ones_f32"]
    identf = env.consts["ident_f"]
    bvrow = env.consts.get("bvrow_b")
    rwb = env.consts.get(rw_name)
    cwt_stg = AR_O[:, 10240:10240 + 256]
    VI = 5
    SI = 2
    TI = 4

    VB = ve0 + CC + 1

    def y_sync(e):
        if with_v:
            for b in range(CC // 2):
                if b >= 1:
                    e.wait_ge(VE.h, ve0 + 2 * b)
                for k in range(2):
                    ci = 2 * b + k
                    e.dma_start(out=AR_O[:, k * 1024:(k + 1) * 1024],
                                in_=wv_f32[ci * 128:(ci + 1) * 128, :]).then_inc(P0.h, 16)
            P0.n += CC * 16
        for m in range(NM):
            sl = m % 2
            if m >= 2:
                e.wait_ge(VE.h, VB + (m - 1) * VI)
            if with_v:
                for cc in range(CC):
                    e.dma_start(
                        out=AR_I[:, (sl * CC + cc) * 128:(sl * CC + cc + 1) * 128],
                        in_=in_bf[cc * 128:(cc + 1) * 128, m * 128:(m + 1) * 128],
                    ).then_inc(Lsl[sl].h, 16)
            for cc in range(CC):
                e.dma_start(
                    out=AR_O[:, 4096 + (sl * CC + cc) * 128:4096 + (sl * CC + cc + 1) * 128],
                    in_=xnf_in[cc * 128:(cc + 1) * 128, m * 128:(m + 1) * 128],
                ).then_inc(Lsl[sl].h, 16)
        npm = (CC if with_v else 0) + CC
        Lsl[0].n += (NM + 1) // 2 * npm * 16
        Lsl[1].n += NM // 2 * npm * 16

    def v_weightcast(e):
        if not with_v:
            e.nop().then_inc(VE.h, CC + 1)
            VE.n += CC + 1
            return
        for b in range(CC // 2):
            e.wait_ge(P0.h, p00 + 16 * (2 * b + 2))
            for k in range(2):
                ci = 2 * b + k
                e.tensor_copy(wsb[:, ci * 1024:(ci + 1) * 1024],
                              AR_O[:, k * 1024:(k + 1) * 1024]).then_inc(VE.h, 1)
        e.drain()
        e.nop().then_inc(VE.h, 1)
        VE.n += CC + 1

    def g_init(e):
        if with_v:
            obt = ob[:, :].tensor
            ap = bass.AP(obt, vsb_off + 64, [[32768, 128], [1040, 2], [65, 16]])
            e.memset(ap, 1.0)
        e.nop().then_inc(env.s("GE").h, 1)
    ge_t = env.s("GE").n + 1
    env.s("GE").n = ge_t

    def t_tensor(e):
        e.wait_ge(VE.h, VB)
        e.wait_ge(env.s("GE").h, ge_t)
        ve_base = VB
        npm = (CC if with_v else 0) + CC
        for m in range(NM):
            sl = m % 2
            e.wait_ge(Lsl[sl].h, l0[sl] + 16 * npm * (m // 2 + 1))
            if m >= 2:
                e.wait_ge(VE.h, ve_base + (m - 1) * VI)
            if m >= 1:
                e.wait_ge(SE.h, se0 + (m - 1) * SI + 1)
            xns = [AR_I[:, (sl * CC + cc) * 128:(sl * CC + cc + 1) * 128] for cc in range(CC)]
            xnf = [AR_O[:, 4096 + (sl * CC + cc) * 128:4096 + (sl * CC + cc + 1) * 128]
                   for cc in range(CC)]
            if with_v:
                for half in range(2):
                    ps = env.bufs[f"ps{half}"]
                    for cc in range(CC):
                        e.matmul(ps[:, :], xns[cc], wsb[:, cc * 1024 + half * 512:cc * 1024 + (half + 1) * 512],
                                 start=(cc == 0), stop=False)
                    e.matmul(ps[:, :], onesb[0:1, :], bvrow[0:1, half * 512:(half + 1) * 512],
                             start=False, stop=True).then_inc(TE.h, 1)
            else:
                e.nop().then_inc(TE.h, 2)
            ps_r = env.bufs["ps2"]
            for cc in range(CC):
                mm = e.matmul(ps_r[0:128, 0:8], xnf[cc], rwb[:, cc * 8:(cc + 1) * 8],
                              start=(cc == 0), stop=(cc == CC - 1))
            mm.then_inc(TE.h, 1)
            e.wait_ge(VE.h, ve_base + m * VI + 4)
            ps_t = env.bufs["ps3"]
            e.transpose(ps_t[0:8, 0:128], AR_S[:, 32:40], identf[:, :]
                        ).then_inc(TE.h, 1)
        TE.n += NM * TI

    def v_vector(e):
        ve_base = VB
        for m in range(NM):
            sl = m % 2
            obt = ob[:, :].tensor
            if with_v:
                for half in range(2):
                    e.wait_ge(TE.h, te0 + m * TI + half + 1)
                    ps = env.bufs[f"ps{half}"]
                    dst = bass.AP(obt, vsb_off + sl * 1040 + half * 520, [[32768, 128], [65, 8], [1, 64]])
                    e.tensor_copy(dst, ps[:, :]).then_inc(VE.h, 1)
            else:
                e.nop().then_inc(VE.h, 2)
            e.wait_ge(TE.h, te0 + m * TI + 3)
            ps_r = env.bufs["ps2"]
            mx = AR_S[:, 16:24]
            e.max(mx, ps_r[0:128, 0:8])
            e.drain()
            e.tensor_scalar(AR_S[:, 24:25], mx[:, 0:1], -1.0, None, ALU.mult
                            ).then_inc(VE.h, 1)
            e.wait_ge(SE.h, se0 + m * SI + 2)
            p = AR_S[:, 40:48]
            mask = AR_S[:, 48:56]
            e.tensor_scalar(mask, p, AR_S[:, 26:27], None, ALU.is_ge)
            e.drain()
            cw = AR_S[:, 32:40]
            e.tensor_scalar(cw, p, AR_S[:, 27:28], None, ALU.mult)
            e.drain()
            e.tensor_tensor(cw, cw, mask, ALU.mult).then_inc(VE.h, 1)
            e.wait_ge(TE.h, te0 + m * TI + 4)
            if m >= 2:
                e.wait_ge(Ssl[sl].h, s0[sl] + (m // 2) * (2 if with_v else 1) * 16)
            ps_t = env.bufs["ps3"]
            e.tensor_copy(cwt_stg[0:8, sl * 128:(sl + 1) * 128], ps_t[0:8, 0:128]
                          ).then_inc(VE.h, 1)
        VE.n += NM * VI

    def s_scalar(e):
        ve_base = VB
        for m in range(NM):
            sl = m % 2
            e.wait_ge(VE.h, ve_base + m * VI + 3)
            ps_r = env.bufs["ps2"]
            p = AR_S[:, 40:48]
            e.activation(p, ps_r[0:128, 0:8], AF.Exp, bias=AR_S[:, 24:25],
                         accum_out=AR_S[:, 25:26]).then_inc(SE.h, 1)
            e.drain()
            e.activation(AR_S[:, 26:27], AR_S[:, 17:18], AF.Exp, bias=AR_S[:, 24:25])
            e.drain()
            if renorm == "top2":
                e.activation(AR_S[:, 27:28], AR_S[:, 26:27], AF.Ln, bias=env.consts["misc"][:, 1:2])
            else:
                e.activation(AR_S[:, 27:28], AR_S[:, 25:26], AF.Ln)
            e.drain()
            e.activation(AR_S[:, 27:28], AR_S[:, 27:28], AF.Exp, scale=-1.0
                         ).then_inc(SE.h, 1)
            if with_v:
                e.wait_ge(VE.h, ve_base + m * VI + 2)
                e.drain()
                e.dma_start(out=vh_out[m * 128:(m + 1) * 128, :],
                            in_=ob[:, vsb_off + sl * 1040:vsb_off + (sl + 1) * 1040]
                            ).then_inc(Ssl[sl].h, 16)
            e.wait_ge(VE.h, ve_base + m * VI + 5)
            e.dma_start(out=cwt_out[0:8, m * 128:(m + 1) * 128],
                        in_=cwt_stg[0:8, sl * 128:(sl + 1) * 128]).then_inc(Ssl[sl].h, 16)
        nst = 2 if with_v else 1
        Ssl[0].n += (NM + 1) // 2 * nst * 16
        Ssl[1].n += NM // 2 * nst * 16
        SE.n += NM * SI

    block.sync(y_sync)
    block.vector(v_weightcast)
    block.gpsimd(g_init)
    block.tensor(t_tensor)
    block.vector(v_vector)
    block.scalar(s_scalar)


def emit_attn(env, block, q_in, k_in, vh_in, a_out):
    TE, VE, SE = (env.s(x) for x in ["TE", "VE", "SE"])
    Lsl = [env.s("L0"), env.s("L1")]
    Ssl = [env.s("S0"), env.s("S1")]
    te0, ve0, se0 = TE.n, VE.n, SE.n
    l0 = [Lsl[0].n, Lsl[1].n]
    s0 = [Ssl[0].n, Ssl[1].n]
    AR_W, AR_I, AR_O, AR_S = (env.bufs[x] for x in ["AR_W", "AR_I", "AR_O", "AR_S"])
    NP = NSEQ * NH
    NTQ = S // 512
    NTK = S // 128
    PAIR = 2048 + 2048 + 1040
    pb = AR_W.bitcast(BF16)
    TI = NTK * 2 + 1
    SI = NTK + 2
    VI = 1

    def y_sync(e):
        for p in range(NP):
            s, h = divmod(p, NH)
            sl = p % 2
            if p >= 2:
                e.wait_ge(VE.h, ve0 + (p - 1) * NTQ * VI)
            base = sl * PAIR
            e.dma_start(out=AR_I[0:64, base:base + 2048],
                        in_=q_in[h * 64:(h + 1) * 64, s * 2048:(s + 1) * 2048]
                        ).then_inc(Lsl[sl].h, 16)
            e.dma_start(out=AR_I[0:64, base + 2048:base + 4096],
                        in_=k_in[h * 64:(h + 1) * 64, s * 2048:(s + 1) * 2048]
                        ).then_inc(Lsl[sl].h, 16)
            for tk in range(NTK):
                e.dma_start(
                    out=AR_I[:, base + 4096 + tk * 65:base + 4096 + (tk + 1) * 65],
                    in_=vh_in[s * 2048 + tk * 128:s * 2048 + (tk + 1) * 128,
                              h * 65:(h + 1) * 65]).then_inc(Lsl[sl].h, 16)
        Lsl[0].n += (NP + 1) // 2 * (2 + NTK) * 16
        Lsl[1].n += NP // 2 * (2 + NTK) * 16

    def t_tensor(e):
        for p in range(NP):
            s, h = divmod(p, NH)
            sl = p % 2
            base = sl * PAIR
            e.wait_ge(Lsl[sl].h, l0[sl] + 16 * (2 + NTK) * (p // 2 + 1))
            q = AR_I[0:64, base:base + 2048]
            k = AR_I[0:64, base + 2048:base + 4096]
            for tq in range(NTQ):
                gq = p * NTQ + tq
                tb = te0 + gq * TI
                sb = se0 + gq * SI
                for tk in range(NTK):
                    if tk >= 2:
                        e.wait_ge(SE.h, sb + tk - 1)
                    elif p * NTQ + tq >= 1:
                        e.wait_ge(SE.h, se0 + (p * NTQ + tq - 1) * SI + NTK - 2 + tk + 1)
                    ps = env.bufs[f"ps{tk % 2}"]
                    e.matmul(ps[:, :], k[:, tk * 128:(tk + 1) * 128],
                             q[:, tq * 512:(tq + 1) * 512],
                             start=True, stop=True).then_inc(TE.h, 1)
                for tk in range(NTK):
                    e.wait_ge(SE.h, sb + tk + 1)
                    vh_t = AR_I[:, base + 4096 + tk * 65:base + 4096 + (tk + 1) * 65]
                    e.matmul(env.bufs["ps2"][0:65, :], vh_t,
                             pb[:, (tk % 2) * 512:(tk % 2 + 1) * 512],
                             start=(tk == 0), stop=(tk == NTK - 1)).then_inc(TE.h, 1)
                e.wait_ge(SE.h, sb + NTK + 1)
                if gq >= 1:
                    e.wait_ge(VE.h, ve0 + gq)
                e.matmul(env.bufs["ps3"][0:64, :], env.consts["ones_f32"][0:1, 0:64],
                         AR_S[0:1, 0:512], start=True, stop=True).then_inc(TE.h, 1)
        TE.n += NP * NTQ * TI

    def s_scalar(e):
        for p in range(NP):
            s, h = divmod(p, NH)
            sl = p % 2
            for tq in range(NTQ):
                gq = p * NTQ + tq
                tb = te0 + gq * TI
                sb = se0 + gq * SI
                for tk in range(NTK):
                    e.wait_ge(TE.h, tb + tk + 1)
                    if gq >= 1 and tk < 2:
                        e.wait_ge(TE.h, te0 + (gq - 1) * TI + NTK + NTK - 2 + tk + 1)
                    ps = env.bufs[f"ps{tk % 2}"]
                    e.activation(pb[:, (tk % 2) * 512:(tk % 2 + 1) * 512], ps[:, :],
                                 AF.Exp, scale=0.125).then_inc(SE.h, 1)
                e.wait_ge(TE.h, tb + 2 * NTK)
                e.activation(AR_S[0:1, 0:512], env.bufs["ps2"][64:65, :], AF.Ln)
                e.drain()
                e.activation(AR_S[0:1, 0:512], AR_S[0:1, 0:512], AF.Exp, scale=-1.0
                             ).then_inc(SE.h, 1)
                e.drain()
                e.activation(AR_O[0:64, 0:512], env.bufs["ps2"][0:64, :], AF.Identity
                             ).then_inc(SE.h, 1)
        SE.n += NP * NTQ * SI

    def v_vector(e):
        for p in range(NP):
            s, h = divmod(p, NH)
            sl = p % 2
            for tq in range(NTQ):
                gq = p * NTQ + tq
                ssl = gq % 2
                e.wait_ge(TE.h, te0 + gq * TI + 2 * NTK + 1)
                e.wait_ge(SE.h, se0 + gq * SI + NTK + 2)
                if gq >= 2:
                    e.wait_ge(Ssl[ssl].h, s0[ssl] + (gq // 2) * 16)
                dst = AR_O.bitcast(BF16)[0:64, 16384 + ssl * 512:16384 + (ssl + 1) * 512]
                e.tensor_tensor(dst, AR_O[0:64, 0:512], env.bufs["ps3"][0:64, :],
                                ALU.mult).then_inc(VE.h, 1)
        VE.n += NP * NTQ * VI

    def g_store(e):
        GD = env.s("GD")
        for p in range(NP):
            s, h = divmod(p, NH)
            for tq in range(NTQ):
                gq = p * NTQ + tq
                ssl = gq % 2
                e.wait_ge(VE.h, ve0 + gq + 1)
                src = AR_O.bitcast(BF16)[0:64, 16384 + ssl * 512:16384 + (ssl + 1) * 512]
                e.dma_start(out=a_out[h * 64:(h + 1) * 64,
                                      s * 2048 + tq * 512:s * 2048 + (tq + 1) * 512],
                            in_=src).then_inc(Ssl[ssl].h, 16)
        Ssl[0].n += (NP * NTQ + 1) // 2 * 16
        Ssl[1].n += NP * NTQ // 2 * 16

    block.sync(y_sync)
    block.tensor(t_tensor)
    block.scalar(s_scalar)
    block.vector(v_vector)
    block.gpsimd(g_store)


def emit_moe(env, block, in_bf, w1_ap, w2_ap, f_dim, cwt, acc, act, b1=None):
    TE, VE, SE = (env.s(x) for x in ["TE", "VE", "SE"])
    P0, X0, X1s = env.s("P0"), env.s("X0"), env.s("X1")
    Lsl = [env.s("L0"), env.s("L1")]
    Ssl = [env.s("S0"), env.s("S1")]
    te0, ve0, se0, p00, x00, x10 = TE.n, VE.n, SE.n, P0.n, X0.n, X1s.n
    l0 = [Lsl[0].n, Lsl[1].n]
    s0 = [Ssl[0].n, Ssl[1].n]
    AR_W, AR_I, AR_O, AR_S = (env.bufs[x] for x in ["AR_W", "AR_I", "AR_O", "AR_S"])
    FP = 2048
    NFH = f_dim // FP
    NPASS = NE * NFH
    NFT = FP // 128
    b1_t = env.consts[b1] if b1 is not None else None
    actf = AF.Silu if act == "silu" else AF.Gelu_apprx_tanh
    TI = NFT + CC + 1
    SI = NFT
    VI = 1 + 1 + CC * 2
    VPREP = CC + NFT + 1

    def pass_bases(p):
        return (te0 + p * (TT * TI), se0 + p * (TT * SI),
                ve0 + p * (VPREP + TT * VI))

    def y_sync(e):
        for p in range(NPASS):
            ee, fh = divmod(p, NFH)
            tb, sb, vb = pass_bases(p)
            if p >= 1:
                e.wait_ge(VE.h, vb)
                psl = (p - 1) % 2
                e.wait_ge(Ssl[0].h, s0[0] + p * TT // 2 * CC * 16)
                e.wait_ge(Ssl[1].h, s0[1] + p * TT // 2 * CC * 16)
            for b in range(CC // 2):
                if b >= 1:
                    e.wait_ge(VE.h, vb + 2 * b)
                for k in range(2):
                    ci = 2 * b + k
                    e.dma_start(out=AR_O[:, k * 2048:(k + 1) * 2048],
                                in_=w1_ap(ee)[ci * 128:(ci + 1) * 128, fh * FP:(fh + 1) * FP]
                                ).then_inc(P0.h, 16)
            for b in range(NFT // 2):
                if b >= 1 or True:
                    e.wait_ge(VE.h, vb + CC + 2 * b)
                for k in range(2):
                    fi = 2 * b + k
                    e.dma_start(out=AR_O[:, 4096 + k * 1024:4096 + (k + 1) * 1024],
                                in_=w2_ap(ee)[fh * FP + fi * 128:fh * FP + (fi + 1) * 128, :]
                                ).then_inc(P0.h, 16)
            for tt in range(TT):
                sl = tt % 2
                if tt >= 2:
                    e.wait_ge(TE.h, tb + (tt - 1) * TI)
                    e.wait_ge(VE.h, vb + VPREP + (tt - 1) * VI)
                    e.wait_ge(Ssl[sl].h, s0[sl] + (p * (TT // 2) + tt // 2) * CC * 16)
                else:
                    e.wait_ge(VE.h, vb + VPREP)
                for cc in range(CC):
                    e.dma_start(
                        out=AR_I[:, (sl * CC + cc) * 512:(sl * CC + cc + 1) * 512],
                        in_=in_bf[cc * 128:(cc + 1) * 128, tt * 512:(tt + 1) * 512],
                    ).then_inc(Lsl[sl].h, 16)
                for co in range(CC):
                    e.dma_start(
                        out=AR_O[:, (sl * CC + co) * 512:(sl * CC + co + 1) * 512],
                        in_=acc[co * 128:(co + 1) * 128, tt * 512:(tt + 1) * 512],
                    ).then_inc(X0.h if sl == 0 else X1s.h, 16)
                e.dma_start(out=AR_S[0:1, 1024 + sl * 512:1024 + (sl + 1) * 512],
                            in_=cwt[ee:ee + 1, tt * 512:(tt + 1) * 512]
                            ).then_inc(Lsl[sl].h, 16)
        Lsl[0].n += NPASS * ((TT + 1) // 2) * (CC + 1) * 16
        Lsl[1].n += NPASS * (TT // 2) * (CC + 1) * 16
        X0.n += NPASS * ((TT + 1) // 2) * CC * 16
        X1s.n += NPASS * (TT // 2) * CC * 16
        P0.n += NPASS * (CC + NFT) * 16

    def v_vector(e):
        for p in range(NPASS):
            ee, fh = divmod(p, NFH)
            tb, sb, vb = pass_bases(p)
            pbase = p00 + p * (CC + NFT) * 16
            for b in range(CC // 2):
                e.wait_ge(P0.h, pbase + 16 * (2 * b + 2))
                for k in range(2):
                    ci = 2 * b + k
                    e.tensor_copy(AR_W[:, ci * 2048:(ci + 1) * 2048],
                                  AR_O[:, k * 2048:(k + 1) * 2048]).then_inc(VE.h, 1)
            for b in range(NFT // 2):
                e.wait_ge(P0.h, pbase + 16 * (CC + 2 * b + 2))
                for k in range(2):
                    fi = 2 * b + k
                    e.tensor_copy(AR_W[:, 16384 + fi * 1024:16384 + (fi + 1) * 1024],
                                  AR_O[:, 4096 + k * 1024:4096 + (k + 1) * 1024]
                                  ).then_inc(VE.h, 1)
            e.drain()
            e.nop().then_inc(VE.h, 1)
            for tt in range(TT):
                sl = tt % 2
                e.wait_ge(TE.h, tb + tt * TI + NFT + 1)
                cwb = AR_O[:, 8192 + sl * 512:8192 + (sl + 1) * 512]
                e.tensor_copy(cwb, env.bufs["ps4"][:, :]).then_inc(VE.h, 1)
                e.drain()
                e.nop().then_inc(VE.h, 1)
                for co in range(CC):
                    gy = tb + tt * TI + NFT + 1 + co + 1
                    e.wait_ge(TE.h, gy)
                    ps = env.bufs[f"ps{2 + co % 2}"]
                    acct = AR_O[:, (sl * CC + co) * 512:(sl * CC + co + 1) * 512]
                    tmp = AR_S[:, (co % 2) * 512:(co % 2 + 1) * 512]
                    e.drain()
                    e.tensor_tensor(tmp, cwb, ps[:, :], ALU.mult)
                    e.drain()
                    e.tensor_tensor(acct, acct, tmp, ALU.add).then_inc(VE.h, 2)
        VE.n += NPASS * (VPREP + TT * VI)

    def t_tensor(e):
        for p in range(NPASS):
            ee, fh = divmod(p, NFH)
            tb, sb, vb = pass_bases(p)
            e.wait_ge(VE.h, vb + VPREP)
            for tt in range(TT):
                sl = tt % 2
                e.wait_ge(Lsl[sl].h, l0[sl] + 16 * (CC + 1) * (p * TT // 2 + tt // 2 + 1))
                for fi in range(NFT):
                    if tt > 0 or fi >= 2:
                        gprev = tt * NFT + fi - 2 if fi >= 2 else (tt - 1) * NFT + NFT - 2 + fi
                        e.wait_ge(SE.h, sb + gprev + 1)
                    ps = env.bufs[f"ps{fi % 2}"]
                    for cc in range(CC):
                        mm = e.matmul(ps[:, :],
                                      AR_W[:, cc * 2048 + fi * 128:cc * 2048 + (fi + 1) * 128],
                                      AR_I[:, (sl * CC + cc) * 512:(sl * CC + cc + 1) * 512],
                                      start=(cc == 0), stop=(cc == CC - 1))
                    mm.then_inc(TE.h, 1)
                XA = X0 if sl == 0 else X1s
                xa0 = x00 if sl == 0 else x10
                e.wait_ge(XA.h, xa0 + 16 * CC * (p * TT // 2 + tt // 2 + 1))
                if tt >= 1:
                    e.wait_ge(VE.h, vb + VPREP + (tt - 1) * VI + 1)
                elif p >= 1:
                    e.wait_ge(VE.h, vb - TT * VI + (TT - 1) * VI + 1)
                e.matmul(env.bufs["ps4"][:, :], env.consts["ones_f32"][0:1, :],
                         AR_S[0:1, 1024 + sl * 512:1024 + (sl + 1) * 512],
                         start=True, stop=True).then_inc(TE.h, 1)
                for co in range(CC):
                    gy = tt * TI + NFT + 1 + co
                    if p * TT * TI + gy >= 2:
                        pass
                    if co >= 2:
                        e.wait_ge(VE.h, vb + VPREP + tt * VI + 2 + (co - 2) * 2 + 2)
                    elif tt >= 1:
                        e.wait_ge(VE.h, vb + VPREP + (tt - 1) * VI + 2 + (CC - 2 + co) * 2 + 2)
                    elif p >= 1:
                        e.wait_ge(VE.h, vb - TT * VI + (TT - 1) * VI + 2 + (CC - 2 + co) * 2 + 2)
                    ps = env.bufs[f"ps{2 + co % 2}"]
                    for fi in range(NFT):
                        e.wait_ge(SE.h, sb + tt * NFT + fi + 1)
                        mm = e.matmul(ps[:, :],
                                      AR_W[:, 16384 + fi * 1024 + co * 128:16384 + fi * 1024 + (co + 1) * 128],
                                      AR_I[:, 8192 + fi * 512:8192 + (fi + 1) * 512],
                                      start=(fi == 0), stop=(fi == NFT - 1))
                    mm.then_inc(TE.h, 1)
        TE.n += NPASS * TT * TI

    def s_scalar(e):
        for p in range(NPASS):
            ee, fh = divmod(p, NFH)
            tb, sb, vb = pass_bases(p)
            for tt in range(TT):
                sl = tt % 2
                for fi in range(NFT):
                    g = tt * NFT + fi
                    e.wait_ge(TE.h, tb + tt * TI + fi + 1)
                    if tt >= 1:
                        e.wait_ge(TE.h, tb + (tt - 1) * TI + TI)
                    ps = env.bufs[f"ps{fi % 2}"]
                    dst = AR_I[:, 8192 + fi * 512:8192 + (fi + 1) * 512]
                    if b1_t is not None:
                        ins = e.activation(dst, ps[:, :], actf,
                                           bias=b1_t[:, ee * (f_dim // 128) + fh * NFT + fi:
                                                     ee * (f_dim // 128) + fh * NFT + fi + 1])
                    else:
                        ins = e.activation(dst, ps[:, :], actf)
                    ins.then_inc(SE.h, 1)
                for co in range(CC):
                    e.wait_ge(VE.h, vb + VPREP + tt * VI + 2 + co * 2 + 2)
                    acct = AR_O[:, (sl * CC + co) * 512:(sl * CC + co + 1) * 512]
                    e.dma_start(out=acc[co * 128:(co + 1) * 128, tt * 512:(tt + 1) * 512],
                                in_=acct).then_inc(Ssl[sl].h, 16)
        SE.n += NPASS * TT * SI
        Ssl[0].n += NPASS * ((TT + 1) // 2) * CC * 16
        Ssl[1].n += NPASS * (TT // 2) * CC * 16

    block.sync(y_sync)
    block.vector(v_vector)
    block.tensor(t_tensor)
    block.scalar(s_scalar)


SCAN_L = 128
SCAN_NCH = S // SCAN_L


def _scan_prelude(env, e_scalar_ops, which):
    pass


def emit_scan_consts(env, block):
    VE, SE = env.s("VE"), env.s("SE")
    AR_S = env.bufs["AR_S"]

    def s_(e):
        e.wait_ge(VE.h, VE.n)
        e.activation(AR_S[:, 16:24], env.consts["decay_t"][:, :], AF.Exp, scale=1.0)
        e.drain()
        e.activation(AR_S[:, 0:8], AR_S[:, 16:24], AF.Exp, scale=-1.0)
        e.drain()
        e.activation(AR_S[:, 8:16], env.consts["u_t"][:, :], AF.Exp).then_inc(SE.h, 1)
        SE.n += 1
    block.scalar(s_)
    starget = env.s("SE").n

    def v_(e):
        e.wait_ge(SE.h, starget)
        e.memset(AR_S[:, 32:64], 0.0)
        e.memset(AR_S[:, 64:96], 0.0)
        e.drain()
        e.nop().then_inc(VE.h, 1)
        VE.n += 1
    block.vector(v_)
    vtarget = env.s("VE").n

    def w(e):
        e.wait_ge(VE.h, vtarget)
    for reg in [block.sync, block.tensor, block.gpsimd]:
        reg(w)


def _chunk_load(e, dst_base, src, ch, sem):
    for cc in range(CC):
        e.dma_start(
            out=dst_base[:, cc * 512:(cc + 1) * 512],
            in_=bass.AP(src[:, :].tensor, cc * 128 * T + ch * SCAN_L,
                        [[T, 128], [S, NSEQ], [1, SCAN_L]]),
        ).then_inc(sem.h, 16)


def _chunk_store(e, src_base, dst, ch, sem, bf=False):
    for cc in range(CC):
        e.dma_start(
            out=bass.AP(dst[:, :].tensor, cc * 128 * T + ch * SCAN_L,
                        [[T, 128], [S, NSEQ], [1, SCAN_L]]),
            in_=src_base[:, cc * 512:(cc + 1) * 512],
        ).then_inc(sem.h, 16)


def emit_scan_fwd(env, block, k_in, v_in, nf_out, df_out):
    VE, SE = env.s("VE"), env.s("SE")
    Lsl = [env.s("L0"), env.s("L1")]
    Ssl = [env.s("S0"), env.s("S1")]
    ve0, se0 = VE.n, SE.n
    l0 = [Lsl[0].n, Lsl[1].n]
    s0 = [Ssl[0].n, Ssl[1].n]
    AR_W, AR_I, AR_O, AR_S = (env.bufs[x] for x in ["AR_W", "AR_I", "AR_O", "AR_S"])
    WF = AR_W.bitcast(F32)
    IF = AR_I.bitcast(F32)
    kv_slot = lambda sl, which: AR_O[:, (sl * 2 + which) * 4096:(sl * 2 + which + 1) * 4096]
    kap = IF[:, 0:4096]
    kvp = IF[:, 4096:8192]
    nump = IF[:, 8192:12288]
    denp = WF[:, 8256:12352]
    abuf = WF[:, 0:4128]
    bbuf = WF[:, 4128:8256]
    ew = AR_S[:, 0:8]
    eu = AR_S[:, 8:16]
    carryA = AR_S[:, 32:64]
    carryB = AR_S[:, 64:96]
    VI, SI = 3, 1

    def y_sync(e):
        for ch in range(SCAN_NCH):
            sl = ch % 2
            if ch >= 2:
                e.wait_ge(SE.h, se0 + ch - 1)
                e.wait_ge(VE.h, ve0 + (ch - 2) * VI + 1)
            _chunk_load(e, kv_slot(sl, 0), k_in, ch, Lsl[sl])
            _chunk_load(e, kv_slot(sl, 1), v_in, ch, Lsl[sl])
        Lsl[0].n += (SCAN_NCH + 1) // 2 * 16 * 16
        Lsl[1].n += SCAN_NCH // 2 * 16 * 16

    def s_scalar(e):
        for ch in range(SCAN_NCH):
            sl = ch % 2
            e.wait_ge(Lsl[sl].h, l0[sl] + (ch // 2 + 1) * 16 * 16)
            if ch >= 1:
                e.wait_ge(VE.h, ve0 + ch * VI)
            e.activation(kap, kv_slot(sl, 0), AF.Exp).then_inc(SE.h, 1)
            if ch >= 1:
                e.wait_ge(VE.h, ve0 + (ch - 1) * VI + 3)
                _chunk_store(e, nump, nf_out, ch - 1, Ssl[(ch - 1) % 2])
                _chunk_store(e, denp, df_out, ch - 1, Ssl[(ch - 1) % 2])
        e.wait_ge(VE.h, ve0 + (SCAN_NCH - 1) * VI + 3)
        _chunk_store(e, nump, nf_out, SCAN_NCH - 1, Ssl[(SCAN_NCH - 1) % 2])
        _chunk_store(e, denp, df_out, SCAN_NCH - 1, Ssl[(SCAN_NCH - 1) % 2])
        SE.n += SCAN_NCH * SI
        Ssl[0].n += (SCAN_NCH + 1) // 2 * 16 * 16
        Ssl[1].n += SCAN_NCH // 2 * 16 * 16

    def v_vector(e):
        for ch in range(SCAN_NCH):
            sl = ch % 2
            e.wait_ge(SE.h, se0 + ch + 1)
            e.tensor_tensor(kvp, kap, kv_slot(sl, 1), ALU.mult).then_inc(VE.h, 1)
            e.drain()
            e.tensor_copy(bass.AP(abuf[:, 0:1].tensor,
                                  abuf[:, 0:1].offset, [[16384, 128], [129, 32]]), carryA)
            e.tensor_copy(bass.AP(bbuf[:, 0:1].tensor, bbuf[:, 0:1].offset, [[16384, 128], [129, 32]]), carryB)
            e.drain()
            for slot in range(32):
                cc = slot // 4
                e.tensor_tensor_scan(
                    bass.AP(abuf[:, 0:1].tensor, abuf[:, 0:1].offset + slot * 129 + 1, [[16384, 128], [1, SCAN_L]]),
                    ew[:, cc:cc + 1].broadcast_to((128, SCAN_L)),
                    kvp[:, slot * SCAN_L:(slot + 1) * SCAN_L],
                    carryA[:, slot:slot + 1], ALU.mult, ALU.add)
            for slot in range(32):
                cc = slot // 4
                e.tensor_tensor_scan(
                    bass.AP(bbuf[:, 0:1].tensor, bbuf[:, 0:1].offset + slot * 129 + 1, [[16384, 128], [1, SCAN_L]]),
                    ew[:, cc:cc + 1].broadcast_to((128, SCAN_L)),
                    kap[:, slot * SCAN_L:(slot + 1) * SCAN_L],
                    carryB[:, slot:slot + 1], ALU.mult, ALU.add)
            e.drain()
            e.tensor_copy(carryA, bass.AP(abuf[:, 0:1].tensor, abuf[:, 0:1].offset + 128, [[16384, 128], [129, 32]]))
            e.tensor_copy(carryB, bass.AP(bbuf[:, 0:1].tensor, bbuf[:, 0:1].offset + 128, [[16384, 128], [129, 32]])
                          ).then_inc(VE.h, 1)
            e.drain()
            if ch >= 2:
                e.wait_ge(Ssl[sl].h, s0[sl] + (ch // 2) * 16 * 16)
            for cc in range(CC):
                av = bass.AP(abuf[:, 0:1].tensor, abuf[:, 0:1].offset + cc * 4 * 129, [[16384, 128], [129, 4], [1, SCAN_L]])
                e.scalar_tensor_tensor(nump[:, cc * 512:(cc + 1) * 512],
                                       kvp[:, cc * 512:(cc + 1) * 512], eu[:, cc:cc + 1],
                                       av, ALU.mult, ALU.add)
            for cc in range(CC):
                bv = bass.AP(bbuf[:, 0:1].tensor, bbuf[:, 0:1].offset + cc * 4 * 129, [[16384, 128], [129, 4], [1, SCAN_L]])
                ins = e.scalar_tensor_tensor(denp[:, cc * 512:(cc + 1) * 512],
                                             kap[:, cc * 512:(cc + 1) * 512], eu[:, cc:cc + 1],
                                             bv, ALU.mult, ALU.add)
            ins.then_inc(VE.h, 1)
        VE.n += SCAN_NCH * VI

    block.sync(y_sync)
    block.scalar(s_scalar)
    block.vector(v_vector)


def emit_scan_bwd(env, block, k_in, v_in, nf_in, df_in, r_in, rw_out):
    VE, SE = env.s("VE"), env.s("SE")
    Lsl = [env.s("L0"), env.s("L1")]
    Xsl = [env.s("X0"), env.s("X1")]
    Ssl = [env.s("S0"), env.s("S1")]
    P0 = env.s("P0")
    ve0, se0, p00 = VE.n, SE.n, P0.n
    l0 = [Lsl[0].n, Lsl[1].n]
    x0 = [Xsl[0].n, Xsl[1].n]
    s0 = [Ssl[0].n, Ssl[1].n]
    AR_W, AR_I, AR_O, AR_S = (env.bufs[x] for x in ["AR_W", "AR_I", "AR_O", "AR_S"])
    WF = AR_W.bitcast(F32)
    IF = AR_I.bitcast(F32)
    kv_slot = lambda sl, which: AR_O[:, (sl * 2 + which) * 4096:(sl * 2 + which + 1) * 4096]
    kap = IF[:, 0:4096]
    kvp = IF[:, 4096:8192]
    nump = IF[:, 8192:12288]
    denp = WF[:, 8256:12352]
    abuf = WF[:, 0:4128]
    bbuf = WF[:, 4128:8256]
    rbf = AR_I[:, 0:4096]
    rwst = AR_I[:, 8192:12288]
    ew = AR_S[:, 0:8]
    eu = AR_S[:, 8:16]
    carryA = AR_S[:, 32:64]
    carryB = AR_S[:, 64:96]
    VI = 5
    SI = 2

    def y_sync(e):
        for i in range(SCAN_NCH):
            ch = SCAN_NCH - 1 - i
            sl = i % 2
            if i >= 2:
                e.wait_ge(SE.h, se0 + (i - 1) * SI + 1)
                e.wait_ge(VE.h, ve0 + (i - 1) * VI)
            _chunk_load(e, kv_slot(sl, 0), k_in, ch, Lsl[sl])
            _chunk_load(e, kv_slot(sl, 1), v_in, ch, Lsl[sl])
            e.wait_ge(VE.h, ve0 + i * VI + 3)
            if i >= 1:
                e.wait_ge(Ssl[(i - 1) % 2].h, s0[(i - 1) % 2] + ((i - 1) // 2 + 1) * 8 * 16)
            _chunk_load(e, kap, nf_in, ch, Xsl[sl])
            _chunk_load(e, kvp, df_in, ch, Xsl[sl])
            e.wait_ge(VE.h, ve0 + i * VI + 4)
            for cc in range(CC):
                e.dma_start(
                    out=rbf[:, cc * 512:(cc + 1) * 512],
                    in_=bass.AP(r_in[:, :].tensor, cc * 128 * T + ch * SCAN_L,
                                [[T, 128], [S, NSEQ], [1, SCAN_L]]),
                ).then_inc(Xsl[sl].h, 16)
        Lsl[0].n += (SCAN_NCH + 1) // 2 * 16 * 16
        Lsl[1].n += SCAN_NCH // 2 * 16 * 16
        Xsl[0].n += (SCAN_NCH + 1) // 2 * 24 * 16
        Xsl[1].n += SCAN_NCH // 2 * 24 * 16

    def s_scalar(e):
        for i in range(SCAN_NCH):
            sl = i % 2
            e.wait_ge(Lsl[sl].h, l0[sl] + (i // 2 + 1) * 16 * 16)
            if i >= 1:
                e.wait_ge(VE.h, ve0 + i * VI)
            e.activation(kap, kv_slot(sl, 0), AF.Exp).then_inc(SE.h, 1)
            if i >= 1:
                e.wait_ge(VE.h, ve0 + (i - 1) * VI + 5)
                ch2 = SCAN_NCH - 1 - (i - 1)
                for cc in range(CC):
                    e.dma_start(
                        out=bass.AP(rw_out[:, :].tensor, cc * 128 * T + ch2 * SCAN_L,
                                    [[T, 128], [S, NSEQ], [1, SCAN_L]]),
                        in_=rwst[:, cc * 512:(cc + 1) * 512],
                    ).then_inc(Ssl[(i - 1) % 2].h, 16)
            e.wait_ge(VE.h, ve0 + i * VI + 4)
            if 'norec' in DBG:
                e.nop().then_inc(SE.h, 1)
            else:
                e.activation(denp, denp, AF.Ln)
                e.drain()
                e.activation(denp, denp, AF.Exp, scale=-1.0).then_inc(SE.h, 1)
        e.wait_ge(VE.h, ve0 + (SCAN_NCH - 1) * VI + 5)
        for cc in range(CC):
            e.dma_start(
                out=bass.AP(rw_out[:, :].tensor, cc * 128 * T + 0 * SCAN_L,
                            [[T, 128], [S, NSEQ], [1, SCAN_L]]),
                in_=rwst[:, cc * 512:(cc + 1) * 512],
            ).then_inc(Ssl[(SCAN_NCH - 1) % 2].h, 16)
        SE.n += SCAN_NCH * SI
        Ssl[0].n += (SCAN_NCH + 1) // 2 * 8 * 16
        Ssl[1].n += SCAN_NCH // 2 * 8 * 16

    def v_vector(e):
        e.memset(carryA[:, :], 0.0)
        e.memset(carryB[:, :], 0.0)
        e.drain()
        for i in range(SCAN_NCH):
            ch = SCAN_NCH - 1 - i
            sl = i % 2
            e.wait_ge(SE.h, se0 + i * SI + 1)
            if i >= 1:
                e.wait_ge(Ssl[(i - 1) % 2].h, s0[(i - 1) % 2] + ((i - 1) // 2 + 1) * 8 * 16)
            e.tensor_tensor(kvp, kap, kv_slot(sl, 1), ALU.mult).then_inc(VE.h, 1)
            e.drain()
            if 'noscan' in DBG:
                e.nop().then_inc(VE.h, 1)
                e.nop().then_inc(VE.h, 1)
                continue_scan = False
            else:
                continue_scan = True
            e.tensor_copy(bass.AP(abuf[:, 0:1].tensor, abuf[:, 0:1].offset + 128, [[16384, 128], [129, 32]]), carryA) if continue_scan else None
            e.tensor_copy(bass.AP(bbuf[:, 0:1].tensor, bbuf[:, 0:1].offset + 128, [[16384, 128], [129, 32]]), carryB) if continue_scan else None
            e.drain() if continue_scan else None
            for slot in (range(32) if continue_scan else []):
                cc = slot // 4
                if 'noneg' in DBG:
                    e.tensor_tensor_scan(
                        bass.AP(abuf[:, 0:1].tensor, abuf[:, 0:1].offset + slot * 129, [[16384, 128], [1, SCAN_L]]),
                        ew[:, cc:cc + 1].broadcast_to((128, SCAN_L)),
                        kvp[:, slot * SCAN_L:(slot + 1) * SCAN_L],
                        carryA[:, slot:slot + 1], ALU.mult, ALU.add)
                else:
                    e.tensor_tensor_scan(
                        bass.AP(abuf[:, 0:1].tensor, abuf[:, 0:1].offset + slot * 129 + SCAN_L - 1, [[16384, 128], [-1, SCAN_L]]),
                        ew[:, cc:cc + 1].broadcast_to((128, SCAN_L)),
                        bass.AP(kvp[:, 0:1].tensor, kvp[:, 0:1].offset + slot * SCAN_L + SCAN_L - 1, [[12288, 128], [-1, SCAN_L]]),
                        carryA[:, slot:slot + 1], ALU.mult, ALU.add)
            for slot in (range(32) if continue_scan else []):
                cc = slot // 4
                if 'noneg' in DBG:
                    e.tensor_tensor_scan(
                        bass.AP(bbuf[:, 0:1].tensor, bbuf[:, 0:1].offset + slot * 129, [[16384, 128], [1, SCAN_L]]),
                        ew[:, cc:cc + 1].broadcast_to((128, SCAN_L)),
                        kap[:, slot * SCAN_L:(slot + 1) * SCAN_L],
                        carryB[:, slot:slot + 1], ALU.mult, ALU.add)
                else:
                    e.tensor_tensor_scan(
                        bass.AP(bbuf[:, 0:1].tensor, bbuf[:, 0:1].offset + slot * 129 + SCAN_L - 1, [[16384, 128], [-1, SCAN_L]]),
                        ew[:, cc:cc + 1].broadcast_to((128, SCAN_L)),
                        bass.AP(kap[:, 0:1].tensor, kap[:, 0:1].offset + slot * SCAN_L + SCAN_L - 1, [[12288, 128], [-1, SCAN_L]]),
                        carryB[:, slot:slot + 1], ALU.mult, ALU.add)
            if continue_scan:
                e.drain()
                e.tensor_copy(carryA, bass.AP(abuf[:, 0:1].tensor, abuf[:, 0:1].offset + 0, [[16384, 128], [129, 32]]))
                e.tensor_copy(carryB, bass.AP(bbuf[:, 0:1].tensor, bbuf[:, 0:1].offset + 0, [[16384, 128], [129, 32]])
                              ).then_inc(VE.h, 1)
                e.drain()
                for cc in range(CC):
                    av = bass.AP(abuf[:, 0:1].tensor, abuf[:, 0:1].offset + cc * 4 * 129 + 1, [[16384, 128], [129, 4], [1, SCAN_L]])
                    e.scalar_tensor_tensor(nump[:, cc * 512:(cc + 1) * 512],
                                           kvp[:, cc * 512:(cc + 1) * 512], eu[:, cc:cc + 1],
                                           av, ALU.mult, ALU.add)
                for cc in range(CC):
                    bv = bass.AP(bbuf[:, 0:1].tensor, bbuf[:, 0:1].offset + cc * 4 * 129 + 1, [[16384, 128], [129, 4], [1, SCAN_L]])
                    ins = e.scalar_tensor_tensor(denp[:, cc * 512:(cc + 1) * 512],
                                                 kap[:, cc * 512:(cc + 1) * 512], eu[:, cc:cc + 1],
                                                 bv, ALU.mult, ALU.add)
                ins.then_inc(VE.h, 1)
                e.drain()
            e.wait_ge(Xsl[sl].h, x0[sl] + (i // 2) * 24 * 16 + 16 * 16)
            w1 = kv_slot(sl, 0)[:, 0:4096]
            w2 = kv_slot(sl, 1)[:, 0:4096]
            if 'nomerge' in DBG:
                e.nop().then_inc(VE.h, 1)
            else:
                e.tensor_tensor(w1, kap, denp, ALU.mult)
                e.tensor_tensor(w2, kvp, nump, ALU.mult)
                e.drain()
                e.tensor_tensor(nump, w1, w2, ALU.add)
                e.drain()
                e.tensor_tensor(denp, kvp, denp, ALU.mult).then_inc(VE.h, 1)
                e.drain()
            e.wait_ge(SE.h, se0 + i * SI + 2)
            e.wait_ge(Xsl[sl].h, x0[sl] + (i // 2) * 24 * 16 + 24 * 16)
            if i >= 2:
                e.wait_ge(Ssl[sl].h, s0[sl] + (i // 2) * 8 * 16)
            if 'nomerge' in DBG:
                e.memset(rwst[:, :], 0)
                e.drain()
                e.nop().then_inc(VE.h, 1)
            else:
                e.scalar_tensor_tensor(w1, nump, 0.5, denp, ALU.mult, ALU.mult)
                e.drain()
                e.tensor_tensor(rwst, w1, rbf, ALU.mult).then_inc(VE.h, 1)
        VE.n += SCAN_NCH * VI

    block.sync(y_sync)
    block.scalar(s_scalar)
    block.vector(v_vector)


def emit_barrier_reset(env, block):
    emit_barrier(env, block)
    names = ["TE", "VE", "SE", "GE", "L0", "L1", "S0", "S1", "P0", "X0", "X1", "GD"]
    VE = env.s("VE")

    def v(e):
        for n in names:
            e.sem_clear(env.s(n).h)
        e.nop().then_inc(env.s("BAR").h, 1)
    block.vector(v)
    env.bar_k += 0
    tgt = 5 * env.bar_k + 1

    def w(e):
        e.wait_ge(env.s("BAR").h, tgt)
    for reg in [block.sync, block.tensor, block.scalar, block.gpsimd]:
        reg(w)
    env.bar_k = 0
    def v2(e):
        e.sem_clear(env.s("BAR").h)
        e.nop().then_inc(env.s("GD").h, 1)
    block.vector(v2)

    def w2(e):
        e.wait_ge(env.s("GD").h, 1)
    for reg in [block.sync, block.tensor, block.scalar, block.gpsimd]:
        reg(w2)
    for n in names:
        env.s(n).n = 0
    env.s("BAR").n = 0


def emit_accinit(env, block, x1, a1, cwt, out):
    TE, VE, SE = (env.s(x) for x in ["TE", "VE", "SE"])
    Lsl = [env.s("L0"), env.s("L1")]
    Xsl = [env.s("X0"), env.s("X1")]
    Ssl = [env.s("S0"), env.s("S1")]
    te0, ve0, se0 = TE.n, VE.n, SE.n
    l0 = [Lsl[0].n, Lsl[1].n]
    x0 = [Xsl[0].n, Xsl[1].n]
    s0 = [Ssl[0].n, Ssl[1].n]
    AR_I, AR_O, AR_S = (env.bufs[x] for x in ["AR_I", "AR_O", "AR_S"])
    IF = AR_I.bitcast(F32)
    b2b = env.consts["b2_b"]
    VI = 2 + CC
    TI = CC + 1

    def y_sync(e):
        for tt in range(TT):
            sl = tt % 2
            if tt >= 2:
                e.wait_ge(VE.h, ve0 + (tt - 1) * VI)
            for cc in range(CC):
                e.dma_start(out=IF[:, (sl * CC + cc) * 512:(sl * CC + cc + 1) * 512],
                            in_=x1[cc * 128:(cc + 1) * 128, tt * 512:(tt + 1) * 512]
                            ).then_inc(Lsl[sl].h, 16)
            for cc in range(CC):
                e.dma_start(out=AR_O[:, (sl * CC + cc) * 512:(sl * CC + cc + 1) * 512],
                            in_=a1[cc * 128:(cc + 1) * 128, tt * 512:(tt + 1) * 512]
                            ).then_inc(Xsl[sl].h, 16)
            e.dma_start(out=AR_S[0:8, 512 + sl * 512:512 + (sl + 1) * 512],
                        in_=cwt[0:8, tt * 512:(tt + 1) * 512]).then_inc(Lsl[sl].h, 16)
        Lsl[0].n += (TT + 1) // 2 * (CC + 1) * 16
        Lsl[1].n += TT // 2 * (CC + 1) * 16
        Xsl[0].n += (TT + 1) // 2 * CC * 16
        Xsl[1].n += TT // 2 * CC * 16

    def v_vector(e):
        for tt in range(TT):
            sl = tt % 2
            e.wait_ge(Lsl[sl].h, l0[sl] + (tt // 2 + 1) * (CC + 1) * 16)
            e.wait_ge(Xsl[sl].h, x0[sl] + (tt // 2 + 1) * CC * 16)
            cwb = AR_S.bitcast(BF16)[0:8, 3072 + sl * 512:3072 + (sl + 1) * 512]
            e.tensor_copy(cwb, AR_S[0:8, 512 + sl * 512:512 + (sl + 1) * 512]
                          ).then_inc(VE.h, 1)
            e.drain()
            e.nop().then_inc(VE.h, 1)
            for co in range(CC):
                gidx = tt * CC + co
                e.wait_ge(TE.h, te0 + tt * TI + 1 + co + 1)
                if tt >= 2 and co == 0:
                    e.wait_ge(Ssl[sl].h, s0[sl] + (tt // 2) * CC * 16)
                ps = env.bufs[f"ps{co % 2}"]
                x1t = IF[:, (sl * CC + co) * 512:(sl * CC + co + 1) * 512]
                a1t = AR_O[:, (sl * CC + co) * 512:(sl * CC + co + 1) * 512]
                e.drain()
                e.tensor_tensor(a1t, a1t, ps[:, :], ALU.add)
                e.drain()
                e.tensor_tensor(a1t, a1t, x1t, ALU.add).then_inc(VE.h, 1)
        VE.n += TT * VI

    def t_tensor(e):
        for tt in range(TT):
            sl = tt % 2
            e.wait_ge(VE.h, ve0 + tt * VI + 1)
            e.nop().then_inc(TE.h, 1)
            for co in range(CC):
                if tt * CC + co >= 2:
                    e.wait_ge(VE.h, ve0 + (tt * CC + co - 2) // CC * VI + 2 +
                              (tt * CC + co - 2) % CC + 1)
                ps = env.bufs[f"ps{co % 2}"]
                cwb = AR_S.bitcast(BF16)[0:8, 3072 + sl * 512:3072 + (sl + 1) * 512]
                e.matmul(ps[:, :], b2b[0:8, co * 128:(co + 1) * 128], cwb,
                         start=True, stop=True).then_inc(TE.h, 1)
        TE.n += TT * TI

    def s_store(e):
        for tt in range(TT):
            sl = tt % 2
            for co in range(CC):
                e.wait_ge(VE.h, ve0 + tt * VI + 2 + co + 1)
                a1t = AR_O[:, (sl * CC + co) * 512:(sl * CC + co + 1) * 512]
                e.dma_start(out=out[co * 128:(co + 1) * 128, tt * 512:(tt + 1) * 512],
                            in_=a1t).then_inc(Ssl[sl].h, 16)
        Ssl[0].n += (TT + 1) // 2 * CC * 16
        Ssl[1].n += TT // 2 * CC * 16

    block.sync(y_sync)
    block.vector(v_vector)
    block.tensor(t_tensor)
    block.scalar(s_store)


import ml_dtypes
from concourse.bass_utils import run_bass_kernel_spmd
import contextlib

NCORE = 8


def _build_program():
    nc = bass.Bass("TRN2", target_bir_lowering=False)
    D = {}

    def din(name, shape, dt=F32):
        D[name] = nc.dram_tensor(name, list(shape), dt, kind="ExternalInput").ap()
        return D[name]

    def dtmp(name, shape, dt):
        D[name] = nc.dram_tensor(name, list(shape), dt).ap()
        return D[name]

    x = din("x", [C, T])
    wq = din("wq", [C, C]); wk = din("wk", [C, C]); wv = din("wv", [C, C]); wo = din("wo", [C, C])
    l0w1 = din("l0w1", [NE * C, F0]); l0w2 = din("l0w2", [NE * F0, C])
    wrt = din("wrt", [C, C]); wkt = din("wkt", [C, C]); wvt = din("wvt", [C, C]); wot = din("wot", [C, C])
    wfkt = din("wfkt", [C, F1]); wfvt = din("wfvt", [F1, C]); wfrt = din("wfrt", [C, C])
    l1w1 = din("l1w1", [NE * C, F1]); l1w2 = din("l1w2", [NE * F1, C])
    cmap = {}
    for nm, shape, dt in [
        ("ones_bf", (128, 128), BF16), ("ones_f32", (1, 128), F32),
        ("ident_bf", (128, 128), BF16), ("ident_f", (128, 128), F32),
        ("misc", (128, 16), F32),
        ("b_q", (128, 8), F32), ("b_k", (128, 8), F32), ("b_o", (128, 8), F32),
        ("bvrow_f", (1, 1024), F32),
        ("rw_f", (128, 64), F32), ("gw_f", (128, 64), F32),
        ("g_ln1", (128, 8), F32), ("b_ln1", (128, 8), F32),
        ("g_ln2", (128, 8), F32), ("b_ln2", (128, 8), F32),
        ("decay_t", (128, 8), F32), ("u_t", (128, 8), F32),
        ("b1_t", (128, 256), F32), ("b2_f", (8, 1024), F32),
    ]:
        cmap[nm] = din("c_" + nm, shape, dt)
    builtin = {"ones_bf", "ones_f32", "ident_bf", "misc"}
    extra = [(nm, tuple(cmap[nm].shape), F32) for nm in cmap if nm not in builtin]
    extra += [("bvrow_b", (1, 1024), BF16), ("b2_b", (8, 1024), BF16)]

    xn0 = dtmp("xn0", [C, T], BF16)
    xn0f = dtmp("xn0f", [C, T], F32)
    qq = dtmp("qq", [C, T], BF16)
    kk0 = dtmp("kk0", [C, T], BF16)
    vh = dtmp("vh", [T, 1040], BF16)
    cwt0 = dtmp("cwt0", [8, T], F32)
    acm = dtmp("acm", [C, T], BF16)
    x1 = dtmp("x1", [C, T], F32)
    xn1 = dtmp("xn1", [C, T], BF16)
    xn1f = dtmp("xn1f", [C, T], F32)
    t1 = dtmp("t1", [C, T], BF16)
    cwt1 = dtmp("cwt1", [8, T], F32)
    rg = dtmp("rg", [C, T], BF16)
    k1 = dtmp("k1", [C, T], F32)
    v1 = dtmp("v1", [C, T], F32)
    nf = dtmp("nf", [C, T], F32)
    df = dtmp("df", [C, T], F32)
    rwv = dtmp("rwv", [C, T], BF16)
    xmid = dtmp("xmid", [C, T], F32)
    t2 = dtmp("t2", [C, T], BF16)
    kkx = dtmp("kkx", [F1, T], BF16)
    gg = dtmp("gg", [C, T], BF16)
    a1 = dtmp("a1", [C, T], F32)
    out = nc.dram_tensor("out", [C, T], F32, kind="ExternalOutput").ap()

    env = Env(nc)
    with contextlib.ExitStack() as ctx:
        declare_sems(env, ctx)
        alloc_consts(env, ctx, extra_named=extra)
        alloc_arenas(env, ctx)
        with nc.Block() as block:
            emit_load_consts(env, block, {nm: cmap[nm] for nm in cmap})
            emit_const_casts(env, block, [("bvrow_b", "bvrow_f"), ("b2_b", "b2_f")])
            emit_barrier_reset(env, block)
            emit_ln(env, block, x, xn0, out_f32=xn0f)
            emit_barrier_reset(env, block)
            emit_proj(env, block, xn0, wq, C, C, "bf", qq, bias="b_q")
            emit_barrier_reset(env, block)
            emit_proj(env, block, xn0, wk, C, C, "bf", kk0, bias="b_k")
            emit_barrier_reset(env, block)
            emit_vhat_router(env, block, xn0, wv, vh, cwt0, renorm="top2", xnf_in=xn0f)
            emit_barrier_reset(env, block)
            emit_attn(env, block, qq, kk0, vh, acm)
            emit_barrier_reset(env, block)
            emit_proj(env, block, acm, wo, C, C, "res_bias", x1, bias="b_o", resid=x)
            emit_barrier_reset(env, block)
            emit_moe(env, block, xn0,
                     lambda e: l0w1[e * C:(e + 1) * C, :],
                     lambda e: l0w2[e * F0:(e + 1) * F0, :],
                     F0, cwt0, x1, "silu")
            emit_barrier_reset(env, block)
            emit_ln(env, block, x1, xn1, out_f32=xn1f)
            emit_barrier_reset(env, block)
            emit_ln(env, block, xn1f, t1, gamma="g_ln1", beta="b_ln1")
            emit_barrier_reset(env, block)
            emit_vhat_router(env, block, None, None, None, cwt1, renorm="sum",
                             xnf_in=xn1f, with_v=False, rw_name="gw_f")
            emit_barrier_reset(env, block)
            emit_proj(env, block, t1, wrt, C, C, "sigmoid_bf", rg)
            emit_barrier_reset(env, block)
            emit_proj(env, block, t1, wkt, C, C, "f32", k1)
            emit_barrier_reset(env, block)
            emit_proj(env, block, t1, wvt, C, C, "f32", v1)
            emit_barrier_reset(env, block)
            emit_scan_consts(env, block)
            emit_scan_fwd(env, block, k1, v1, nf, df)
            emit_barrier_reset(env, block)
            emit_scan_consts(env, block)
            emit_scan_bwd(env, block, k1, v1, nf, df, rg, rwv)
            emit_barrier_reset(env, block)
            emit_proj(env, block, rwv, wot, C, C, "res", xmid, resid=xn1f)
            emit_barrier_reset(env, block)
            emit_ln(env, block, xmid, t2, gamma="g_ln2", beta="b_ln2")
            emit_barrier_reset(env, block)
            emit_proj(env, block, t2, wfkt, C, F1, "relusq_bf", kkx)
            emit_barrier_reset(env, block)
            emit_proj(env, block, t2, wfrt, C, C, "sigmoid_bf", gg)
            emit_barrier_reset(env, block)
            emit_proj(env, block, kkx, wfvt, F1, C, "gate_res", a1,
                      resid=xmid, gate=gg)
            emit_barrier_reset(env, block)
            emit_accinit(env, block, x1, a1, cwt1, out)
            emit_barrier_reset(env, block)
            emit_moe(env, block, xn1,
                     lambda e: l1w1[e * C:(e + 1) * C, :],
                     lambda e: l1w2[e * F1:(e + 1) * F1, :],
                     F1, cwt1, out, "gelu", b1="b1_t")
            S0, S1 = env.s("S0"), env.s("S1")
            t0n, t1n = S0.n, S1.n

            def fin(e):
                e.wait_ge(S0.h, t0n)
                e.wait_ge(S1.h, t1n)
            block.scalar(fin)
    return nc


def _col(v):
    return np.ascontiguousarray(np.asarray(v, np.float32).reshape(-1, 128).T)


def kernel(**inp):
    f32 = np.float32
    x = np.asarray(inp["L0_attn_x"], f32)
    xs = x.reshape(NCORE, NSEQ * S, C).transpose(0, 2, 1)
    w = {k: np.asarray(v, f32) for k, v in inp.items()}
    ins_common = {
        "wq": w["L0_attn_w_q"], "wk": w["L0_attn_w_k"], "wv": w["L0_attn_w_v"],
        "wo": w["L0_attn_w_o"],
        "l0w1": w["L0_ffn_w1"].reshape(NE * C, F0),
        "l0w2": w["L0_ffn_w2"].reshape(NE * F0, C),
        "wrt": np.ascontiguousarray(w["L1_attn_Wr"].T),
        "wkt": np.ascontiguousarray(w["L1_attn_Wk"].T),
        "wvt": np.ascontiguousarray(w["L1_attn_Wv"].T),
        "wot": np.ascontiguousarray(w["L1_attn_Wo"].T),
        "wfkt": np.ascontiguousarray(w["L1_attn_Wfk"].T),
        "wfvt": np.ascontiguousarray(w["L1_attn_Wfv"].T),
        "wfrt": np.ascontiguousarray(w["L1_attn_Wfr"].T),
        "l1w1": np.ascontiguousarray(w["L1_ffn_w1"].transpose(1, 2, 0)).reshape(NE * C, F1),
        "l1w2": np.ascontiguousarray(w["L1_ffn_w2"].transpose(1, 2, 0)).reshape(NE * F1, C),
        "c_ones_bf": np.ones((128, 128), ml_dtypes.bfloat16),
        "c_ones_f32": np.ones((1, 128), f32),
        "c_ident_bf": np.eye(128).astype(ml_dtypes.bfloat16),
        "c_ident_f": np.eye(128).astype(f32),
        "c_misc": np.tile(np.array([[1e-5, 1.0] + [0.0] * 14], f32), (128, 1)),
        "c_b_q": _col(w["L0_attn_b_q"]), "c_b_k": _col(w["L0_attn_b_k"]),
        "c_b_o": _col(w["L0_attn_b_o"]),
        "c_bvrow_f": w["L0_attn_b_v"].reshape(1, C),
        "c_rw_f": np.ascontiguousarray(
            w["L0_ffn_router_w"].reshape(8, 128, 8).transpose(1, 0, 2).reshape(128, 64)),
        "c_gw_f": np.ascontiguousarray(
            w["L1_ffn_gate_w"].T.reshape(8, 128, 8).transpose(1, 0, 2).reshape(128, 64)),
        "c_g_ln1": _col(w["L1_attn_ln1_w"]), "c_b_ln1": _col(w["L1_attn_ln1_b"]),
        "c_g_ln2": _col(w["L1_attn_ln2_w"]), "c_b_ln2": _col(w["L1_attn_ln2_b"]),
        "c_decay_t": _col(w["L1_attn_decay"]), "c_u_t": _col(w["L1_attn_u"]),
        "c_b1_t": np.ascontiguousarray(w["L1_ffn_b1"].reshape(NE * (F1 // 128), 128).T),
        "c_b2_f": w["L1_ffn_b2"],
    }
    nc = _build_program()
    in_maps = [dict(ins_common, x=np.ascontiguousarray(xs[i])) for i in range(NCORE)]
    res = run_bass_kernel_spmd(nc, in_maps, core_ids=list(range(NCORE)))
    outs = [res.results[i]["out"].T.reshape(NSEQ, S, C) for i in range(NCORE)]
    return np.concatenate(outs, axis=0).astype(np.float32)
